# version 24
# baseline (speedup 1.0000x reference)
import numpy as np
import ml_dtypes

import concourse.bass as bass
import concourse.mybir as mybir
from concourse.bass_utils import run_bass_kernel_spmd

F32 = mybir.dt.float32
BF16 = mybir.dt.bfloat16
I32 = mybir.dt.int32
AF = mybir.ActivationFunctionType
ALU = mybir.AluOpType
AX = mybir.AxisListType

B, T, D = 2, 8192, 1024
NH, HD = 16, 64
HID = 3072
EPS = 1e-5
INV127 = float(np.float32(1.0 / 127.0))
DOWN_SCALE = INV127 * INV127
NKT = T // 128
NSLOT = 8


def own_chunk_positions(core):
    q = core % 4
    return [4 * s + (q if s % 2 == 0 else 3 - q) for s in range(NSLOT)]



class Prog:

    ENGS = ("t", "v", "s", "g", "y")

    def __init__(self, serial=False):
        self.stream = {e: [] for e in self.ENGS}
        self.cnt = {}
        self.writer = {}
        self.readers = {}
        self.serial = serial
        self.last_ev = None

    NROT = 16

    def op(self, eng, fn, reads=(), writes=(), dma=False, slow=False):
        if dma:
            idx = self.cnt.get("#d" + eng, 0)
            self.cnt["#d" + eng] = idx + 1
            sem = "d%s%02d" % (eng, idx % self.NROT)
            self.cnt[sem] = self.cnt.get(sem, 0) + 16
        else:
            sem = "c" + eng
            self.cnt[sem] = self.cnt.get(sem, 0) + 1
        ev = (eng, sem, self.cnt[sem], dma or slow)
        deps = []

        def need(p):
            if p is None:
                return
            p_eng, p_sem, p_val, p_unsafe = p
            if p_eng == eng == "t" and not p_unsafe and not dma:
                return
            deps.append((p_sem, p_val))

        if self.serial:
            need(self.last_ev)
        if dma and self.cnt[sem] > 16:
            deps.append((sem, self.cnt[sem] - 16))
        for r in reads:
            need(self.writer.get(r))
        for w in writes:
            need(self.writer.get(w))
            for rd in self.readers.get(w, ()):
                need(rd)
        for w in writes:
            self.writer[w] = ev
            self.readers[w] = []
        for r in reads:
            self.readers.setdefault(r, []).append(ev)
        self.stream[eng].append((fn, deps, sem, self.cnt[sem], dma))
        self.last_ev = ev
        return ev


def _emit_one(P, e, eng, sems):
    waited = {}
    for fn, deps, sem, val, dma in P.stream[e]:
        merged = {}
        for s, v in deps:
            if waited.get(s, 0) < v:
                merged[s] = max(merged.get(s, 0), v)
        for s, v in sorted(merged.items()):
            eng.wait_ge(sems[s], v)
            waited[s] = v
        inst = fn(eng)
        inst.then_inc(sems[sem], 16 if dma else 1)
    for s in sorted(P.cnt):
        if s.startswith("d" + e) and not s.startswith("#"):
            eng.wait_ge(sems[s], P.cnt[s])




def build_program():
    import os
    stage = int(os.environ.get("KSTAGE", "5"))
    serial = os.environ.get("KSERIAL", "0") == "1"
    nc = bass.Bass("TRN2", target_bir_lowering=False)

    x_in = nc.dram_tensor("x", [T, D], F32, kind="ExternalInput").ap()
    xown_in = nc.dram_tensor("xown", [2048, D], F32, kind="ExternalInput").ap()
    w1t = nc.dram_tensor("w1t", [D, 3 * D], F32, kind="ExternalInput").ap()
    w2t = nc.dram_tensor("w2t", [D, D], F32, kind="ExternalInput").ap()
    wgt = nc.dram_tensor("wgt", [D, HID], I32, kind="ExternalInput").ap()
    wut = nc.dram_tensor("wut", [D, HID], I32, kind="ExternalInput").ap()
    wdt = nc.dram_tensor("wdt", [HID, D], I32, kind="ExternalInput").ap()
    ropetab = nc.dram_tensor("ropetab", [128, T + 2048], BF16, kind="ExternalInput").ap()
    maskst = nc.dram_tensor("maskst", [16, 128, 1024], BF16,
                            kind="ExternalInput").ap()
    ident_in = nc.dram_tensor("ident", [128, 128], BF16, kind="ExternalInput").ap()
    ones64_in = nc.dram_tensor("ones64", [1, 64], BF16, kind="ExternalInput").ap()
    y_out = nc.dram_tensor("y", [2048, D], F32, kind="ExternalOutput").ap()

    lnxt_d = nc.dram_tensor("lnxt_d", [8, 128, T], BF16, kind="Internal").ap()
    lnown_d = nc.dram_tensor("lnown_d", [8, 128, 2048], BF16, kind="Internal").ap()
    a_d = nc.dram_tensor("a_d", [8, 128, 2048], BF16, kind="Internal").ap()

    cur = [nc.sbuf_base + 31]
    cur[0] -= cur[0] % 32

    sb_at = {}

    def sb(name, shape, dtype, at=None):
        nbytes = int(np.prod(shape[1:])) * mybir.dt.size(dtype)
        if at is None:
            at = cur[0]
            cur[0] = at + ((nbytes + 31) // 32) * 32
        sb_at[name] = at
        return nc.alloc_sbuf_tensor_at(name, list(shape), dtype, offset=at).ap()

    s_rope = sb("s_rope", [128, T + 2048], BF16)
    s_mask = sb("s_mask", [128, 16, 1024], BF16)
    s_ident = sb("s_ident", [128, 128], BF16)
    s_ones64 = sb("s_ones64", [1, 64], BF16)
    s_stats = sb("s_stats", [128, 32], F32)
    s_lnt = sb("s_lnt", [1, 1024], F32)
    s_x0 = sb("s_x0", [128, D], F32)
    s_x1b = sb("s_x1b", [128, D], F32)
    s_sq = sb("s_sq", [128, D], BF16)
    s_lnbf = sb("s_lnbf", [128, D], BF16)
    s_lnbf2 = sb("s_lnbf2", [128, D], BF16)

    arena = cur[0]
    s_kbuf = [sb("s_kbuf0", [128, T], BF16), sb("s_kbuf1", [128, T], BF16)]
    s_vbuf = sb("s_vbuf", [128, 4 * NKT * 65], BF16)
    s_qbuf = [sb("s_qbuf0", [128, 2048], BF16), sb("s_qbuf1", [128, 2048], BF16)]
    s_w1bf = sb("s_w1bf", [128, 8, 768], BF16)
    s_lnin = sb("s_lnin", [128, 8, 1024], BF16)
    s_lnow = sb("s_lnow", [128, 8, 512], BF16)
    s_stage = sb("s_stage", [128, 8, 512], BF16)
    s_x2 = sb("s_x2", [128, D], F32)
    s_x3 = sb("s_x3", [128, D], F32)
    s_t1 = sb("s_t1", [128, 1024], BF16)
    s_u = sb("s_u", [128, 1024], BF16)
    s_ush = sb("s_ush", [128, 1024], BF16)
    s_p = [sb("s_p%d" % i, [128, 1024], BF16) for i in range(3)]
    s_avt = sb("s_avt", [64, 1024], F32, at=sb_at["s_stage"])
    s_srec = sb("s_srec", [1, 1024], BF16, at=sb_at["s_stage"] + 4096)
    s_asb = sb("s_asb", [64, 1024], BF16, at=sb_at["s_stage"] + 6144)
    ph12_end = cur[0]

    cur[0] = arena
    s_a = [sb("s_a0", [128, 8, 128], BF16), sb("s_a1", [128, 8, 128], BF16)]
    s_w2bf = sb("s_w2bf", [128, 8, 1024], BF16)
    s_x1c = sb("s_x1c", [128, 4 * D], F32)
    s_ln2t = sb("s_ln2t", [128, 8, 512], BF16)
    s_wg = [sb("s_wg%d" % i, [128, 8, 128], BF16) for i in range(4)]
    s_wu = [sb("s_wu%d" % i, [128, 8, 128], BF16) for i in range(4)]
    s_wd = [sb("s_wd0", [128, 6, 512], BF16), sb("s_wd1", [128, 6, 512], BF16)]
    s_hbuf = sb("s_hbuf", [128, 24 * 512], BF16)
    s_silu = sb("s_silu", [128, 512], BF16)
    s_ysb = sb("s_ysb", [128, 4 * D], F32)
    ph3_end = cur[0]
    assert max(ph12_end, ph3_end) <= nc.sbuf_top - 32, (ph12_end, ph3_end, nc.sbuf_top)

    pf = nc.alloc_psum_tensor("pf", [128, 3584], F32).ap()
    pbf = nc.alloc_psum_tensor("pbf", [128, 1024], BF16).ap()

    kq_ps = [pf[:, 0:1024], pf[:, 1024:2048]]
    v_ps = [pf[:, 2048:2304], pf[:, 2560:2816], pf[:, 3072:3328]]
    sc_ps = [pf[:, 0:1024], pf[:, 1024:2048]]
    av_ps = [pf[0:65, 2048:2560], pf[0:65, 2560:3072]]
    r_ps = pf[0:64, 3072:3584]
    py_ps = pf[:, 0:1024]
    pg_ps = pf[:, 1024:1536]
    pu_ps = pf[:, 1536:2048]
    pd_ps = [pf[:, 1024 + 512 * i: 1536 + 512 * i] for i in range(4)]

    KQB = [("pb0", "pb1"), ("pb2", "pb3")]
    VB = ["pb4", "pb5", "pb6"]
    SCB = [("pb0", "pb1"), ("pb2", "pb3")]
    AVB = ("pb4", "pb5")
    RB = "pb6"
    TRB = "pb7"

    P = Prog(serial=serial)

    P.op("y", lambda e: e.dma_start(out=s_rope, in_=ropetab), writes=["rope"], dma=True)
    P.op("y", lambda e: e.dma_start(
        out=s_mask, in_=maskst.rearrange("r p x -> p r x")), writes=["mask"], dma=True)
    P.op("y", lambda e: e.dma_start(out=s_ident, in_=ident_in), writes=["ident"], dma=True)
    P.op("y", lambda e: e.dma_start(out=s_ones64, in_=ones64_in), writes=["ones64"], dma=True)

    def _stat(p, k):
        i = ["sumx", "sumsq", "m", "var", "msq", "rstd", "lnv", "negmr"].index(k)
        return s_stats[:, 8 * p + i:8 * p + i + 1]

    def ln_front(x_ap, xres, p):
        st = lambda k: _stat(p, k)
        rn = lambda k: ("st_" + k, p)
        P.op("v", lambda e: e.reduce_sum(st("sumx"), x_ap, AX.X),
             reads=[xres], writes=[rn("sumx")], slow=True)
        P.op("s", lambda e: e.activation(s_sq, x_ap, AF.Square, accum_out=st("sumsq")),
             reads=[xres], writes=[rn("sumsq"), "sq"], slow=True)

    def ln_back(x_ap, xres, out_ap, out_res, p):
        st = lambda k: _stat(p, k)
        rn = lambda k: ("st_" + k, p)
        P.op("v", lambda e: e.tensor_scalar_mul(st("m"), st("sumx"), 1.0 / D),
             reads=[rn("sumx")], writes=[rn("m")])
        P.op("v", lambda e: e.tensor_tensor(st("msq"), st("m"), st("m"), ALU.mult),
             reads=[rn("m")], writes=[rn("msq")])
        P.op("v", lambda e: e.tensor_scalar(st("var"), st("sumsq"), 1.0 / D, EPS,
                                            ALU.mult, ALU.add),
             reads=[rn("sumsq")], writes=[rn("var")])
        P.op("v", lambda e: e.tensor_tensor(st("var"), st("var"), st("msq"),
                                            ALU.subtract),
             reads=[rn("var"), rn("msq")], writes=[rn("var")])

        P.op("s", lambda e: e.activation(st("lnv"), st("var"), AF.Ln),
             reads=[rn("var")], writes=[rn("lnv")])
        P.op("s", lambda e: e.activation(st("rstd"), st("lnv"), AF.Exp, scale=-0.5),
             reads=[rn("lnv")], writes=[rn("rstd")])
        P.op("v", lambda e: e.tensor_scalar(st("negmr"), st("m"), st("rstd"), -1.0,
                                            ALU.mult, ALU.mult),
             reads=[rn("m"), rn("rstd")], writes=[rn("negmr")])
        P.op("v", lambda e: e.tensor_scalar(out_ap, x_ap, st("rstd"),
                                            st("negmr"), ALU.mult, ALU.add),
             reads=[xres, rn("negmr"), rn("rstd")], writes=[out_res])

    def ln_block(x_ap, xres, out_res, p=0):
        ln_front(x_ap, xres, p)
        ln_back(x_ap, xres, s_lnbf, out_res, p)

    def transpose_half(in_ap, in_res, h):
        def f(e, h=h):
            last = None
            for c in range(4 * h, 4 * h + 4):
                last = e.transpose(pbf[:, 128 * c:128 * (c + 1)],
                                   in_ap[:, 128 * c:128 * (c + 1)], s_ident)
            return last
        P.op("t", f, reads=[in_res, "ident"], writes=[("tr", h)])

    def transpose8(in_ap, in_res):
        def f(e):
            last = None
            for c in range(8):
                last = e.transpose(pbf[:, 128 * c:128 * (c + 1)],
                                   in_ap[:, 128 * c:128 * (c + 1)], s_ident)
            return last
        P.op("t", f, reads=[in_res, "ident"], writes=[("tr", 0), ("tr", 1)])

    ph1_pending = []
    ph1_copy_q = []
    rope_q = []
    xbufs = [s_x0, s_x1b, s_x2, s_x3]

    def ph1_finish(info):
        i, xt, src_dram, dst_dram, res_prefix, feed = info
        b2 = i % 2
        lnb = [s_lnbf, s_lnbf2][b2]
        ln_back(xt, ("x", i % 4), lnb, ("lnbf", b2), p=i % 4)
        slot = i % 4
        grp = i // 4
        lb = grp % 2
        transpose8(lnb, ("lnbf", b2))

        def do_copy(slot=slot, grp=grp, lb=lb, feed=feed, dst_dram=dst_dram,
                    res_prefix=res_prefix):
            if feed:
                P.op("v", lambda e: e.tensor_copy(
                    s_lnin[:, :, 512 * lb + 128 * slot:512 * lb + 128 * (slot + 1)],
                    pbf[:].rearrange("p (dc t) -> p dc t", dc=8)),
                    reads=[("tr", 0), ("tr", 1)], writes=[("lnin", lb)])
                if slot == 3:
                    P.op("y", lambda e: e.dma_start(
                        out=dst_dram.rearrange("dc p t -> p dc t")[:, :, 512 * grp:512 * (grp + 1)],
                        in_=s_lnin[:, :, 512 * lb:512 * (lb + 1)]),
                        reads=[("lnin", lb)],
                        writes=[(res_prefix, grp)], dma=True)
            else:
                P.op("v", lambda e: e.tensor_copy(
                    s_stage[:, :, 128 * slot:128 * (slot + 1)],
                    pbf[:].rearrange("p (dc t) -> p dc t", dc=8)),
                    reads=[("tr", 0), ("tr", 1)], writes=[("stage", slot)])
                if slot == 3:
                    P.op("y", lambda e: e.dma_start(
                        out=dst_dram.rearrange("dc p t -> p dc t")[:, :, 512 * grp:512 * (grp + 1)],
                        in_=s_stage),
                        reads=[("stage", 0), ("stage", 1), ("stage", 2), ("stage", 3)],
                        writes=[(res_prefix, grp)], dma=True)
        ph1_copy_q.append(do_copy)

    def ln_transpose_store_range(lo, hi, src_dram, dst_dram, res_prefix, feed=False):
        def dma_x(i):
            xt = xbufs[i % 4]
            P.op("y", lambda e, i=i, xt=xt: e.dma_start(
                out=xt, in_=src_dram[128 * i:128 * (i + 1), :]),
                writes=[("x", i % 4)], dma=True)
        for i in range(lo, min(lo + 2, hi)):
            dma_x(i)
        for i in range(lo, hi):
            if len(ph1_pending) > 1:
                if ph1_copy_q:
                    ph1_copy_q.pop(0)()
                ph1_finish(ph1_pending.pop(0))
            if i + 2 < hi:
                dma_x(i + 2)
            ln_front(xbufs[i % 4], ("x", i % 4), p=i % 4)
            if feed and rope_q:
                rope_q.pop(0)()
            ph1_pending.append((i, xbufs[i % 4], src_dram, dst_dram,
                                res_prefix, feed))
        while ph1_pending:
            if ph1_copy_q:
                ph1_copy_q.pop(0)()
            ph1_finish(ph1_pending.pop(0))
        while ph1_copy_q:
            ph1_copy_q.pop(0)()


    debug_finish = [None]

    w1r = w1t.rearrange("(dc p) j -> p dc j", p=128)
    ctr = {"kq": 0, "v": 0, "sc": 0}

    def rope_store(dst_ap, dst_res, ps_ap, ps_res, c0, width, defer=None):
        cos = s_rope[0:64, c0:c0 + width]
        sin = s_rope[64:128, c0:c0 + width]
        t1 = s_t1[:, 0:width]
        u = s_u[:, 0:width]
        ush = s_ush[:, 0:width]

        def f1(e):
            e.tensor_tensor(t1[0:64, :], ps_ap[0:64, :], cos, ALU.mult)
            return e.tensor_tensor(t1[64:128, :], ps_ap[64:128, :], cos, ALU.mult)

        def f2(e):
            e.tensor_tensor(u[0:64, :], ps_ap[0:64, :], sin, ALU.mult)
            return e.tensor_tensor(u[64:128, :], ps_ap[64:128, :], sin, ALU.mult)

        def f3(e):
            e.tensor_copy(ush[0:32, :], u[32:64, :])
            e.tensor_copy(ush[32:64, :], u[0:32, :])
            e.tensor_copy(ush[64:96, :], u[96:128, :])
            return e.tensor_copy(ush[96:128, :], u[64:96, :])
        thunks = [
            lambda: P.op("v", f1, reads=[*ps_res, "rope"], writes=["r_t1"]),
            lambda: P.op("v", f2, reads=[*ps_res, "rope"], writes=["r_u"]),
            lambda: P.op("v", f3, reads=["r_u"], writes=["r_ush"]),
            lambda: P.op("v", lambda e: e.tensor_tensor(dst_ap, t1, ush, ALU.add),
                         reads=["r_t1", "r_ush"], writes=[dst_res]),
        ]
        if defer is None:
            for th in thunks:
                th()
        else:
            defer.extend(thunks)

    n_pass = 0 if stage < 2 else (1 if stage < 4 else 4)
    attn_on = stage >= 3
    katt = int(os.environ.get("KATT", "5"))
    kslots = int(os.environ.get("KSLOTS", "99"))
    kquarts = int(os.environ.get("KQUARTS", "999999"))

    def author_w1_loads(g):
        kcol, vcol, qcol = D + 256 * g, 2 * D + 256 * g, 256 * g
        P.op("g", lambda e, c=kcol: e.dma_start(
            out=s_w1bf[:, :, 0:256], in_=w1r[:, :, c:c + 256]),
            writes=["w1k"], dma=True)
        P.op("g", lambda e, c=vcol: e.dma_start(
            out=s_w1bf[:, :, 256:512], in_=w1r[:, :, c:c + 256]),
            writes=["w1v"], dma=True)
        P.op("g", lambda e, c=qcol: e.dma_start(
            out=s_w1bf[:, :, 512:768], in_=w1r[:, :, c:c + 256]),
            writes=["w1q"], dma=True)

    P.op("g", lambda e: e.memset(s_vbuf, 1.0), writes=["vbuf"], slow=True)
    author_w1_loads(0)

    def author_qkv_jm(g, jm, resident=False):
        if resident:
            while rope_q:
                rope_q.pop(0)()
        if not resident:
            for half in range(2):
                j = 2 * jm + half
                lb = j % 2
                P.op("y", lambda e, j=j, lb=lb: e.dma_start(
                    out=s_lnin[:, :, 512 * lb:512 * (lb + 1)],
                    in_=lnxt_d.rearrange("dc p t -> p dc t")[:, :, 512 * j:512 * (j + 1)]),
                    reads=[("lnxt", j)], writes=[("lnin", lb)], dma=True)
        if True:
            for half in range(2):
                for ts_ in range(4):
                    vpar = ctr["v"] % 3
                    ctr["v"] += 1
                    kt = 8 * jm + 4 * half + ts_

                    def vmm(e, half=half, ts_=ts_, vpar=vpar):
                        last = None
                        for dc in range(8):
                            last = e.matmul(
                                v_ps[vpar],
                                s_lnin[:, dc, 512 * half + 128 * ts_:
                                       512 * half + 128 * (ts_ + 1)],
                                s_w1bf[:, dc, 256:512],
                                start=(dc == 0), stop=(dc == 7))
                        return last
                    P.op("t", vmm, reads=[("lnin", half), "w1v"], writes=[VB[vpar]])
                    P.op("v", lambda e, vpar=vpar, kt=kt: e.tensor_copy(
                        s_vbuf[:].rearrange("p (h k d) -> p h k d", h=4, k=NKT, d=65)[:, :, kt, 0:64],
                        v_ps[vpar][:].rearrange("p (h d) -> p h d", h=4)),
                        reads=[VB[vpar]], writes=["vbuf"])
            kq_pars = []
            for hp in range(2):
                par = ctr["kq"] % 2
                ctr["kq"] += 1
                kq_pars.append(par)

                def kmm(e, hp=hp, par=par):
                    last = None
                    for dc in range(8):
                        for half in range(2):
                            last = e.matmul(
                                kq_ps[par][:, 512 * half:512 * (half + 1)],
                                s_w1bf[:, dc, hp * 128:(hp + 1) * 128],
                                s_lnin[:, dc, 512 * half:512 * (half + 1)],
                                start=(dc == 0), stop=(dc == 7))
                    return last
                P.op("t", kmm, reads=[("lnin", 0), ("lnin", 1), "w1k"],
                     writes=[*KQB[par]])
            for hp in range(2):
                rope_store(s_kbuf[hp][:, 1024 * jm:1024 * (jm + 1)], ("kbuf", hp),
                           kq_ps[kq_pars[hp]], KQB[kq_pars[hp]], 1024 * jm, 1024,
                           defer=(rope_q if resident else None))
    def author_qblock_sm(g, sm, resident=False):
        if resident:
            while rope_q:
                rope_q.pop(0)()
        if not resident:
            for half in range(2):
                ss = 2 * sm + half
                P.op("y", lambda e, ss=ss, half=half: e.dma_start(
                    out=s_lnow[:, :, 256 * half:256 * (half + 1)],
                    in_=lnown_d.rearrange("dc p t -> p dc t")[:, :, 256 * ss:256 * (ss + 1)]),
                    reads=[("lnown", ss // 2)], writes=[("lnow", half)], dma=True)
        lb = sm % 2
        for hp in range(2):
            par = ctr["kq"] % 2
            ctr["kq"] += 1

            def qmm(e, hp=hp, par=par, resident=resident, lb=lb):
                last = None
                for dc in range(8):
                    rhs = (s_lnin[:, dc, 512 * lb:512 * (lb + 1)]
                           if resident else s_lnow[:, dc, :])
                    last = e.matmul(
                        kq_ps[par][:, 0:512],
                        s_w1bf[:, dc, 512 + hp * 128:512 + (hp + 1) * 128],
                        rhs,
                        start=(dc == 0), stop=(dc == 7))
                return last
            P.op("t", qmm,
                 reads=([("lnin", lb)] if resident else [("lnow", 0), ("lnow", 1)]) + ["w1q"],
                 writes=[*KQB[par]])
            rope_store(s_qbuf[hp][:, 512 * sm:512 * (sm + 1)], ("qbuf", hp),
                       kq_ps[par][:, 0:512], KQB[par], T + 512 * sm, 512,
                       defer=(rope_q if resident else None))

    def author_qblock(g):
        for sm in range(4):
            author_qblock_sm(g, sm)

    feed = n_pass > 0
    for jm in range(8):
        ln_transpose_store_range(8 * jm, 8 * jm + 8, x_in, lnxt_d, "lnxt", feed=feed)
        if n_pass > 0:
            author_qkv_jm(0, jm, resident=True)
    if n_pass > 0:
        for sm in range(4):
            ln_transpose_store_range(4 * sm, 4 * sm + 4, xown_in, lnown_d, "lnown",
                                     feed=True)
            author_qblock_sm(0, sm, resident=True)
        while rope_q:
            rope_q.pop(0)()
    else:
        ln_transpose_store_range(0, 16, xown_in, lnown_d, "lnown")

    for g in range(n_pass):
        if g > 0:
            for jm in range(8):
                author_qkv_jm(g, jm)
            author_qblock(g)

        pending_tail = []

        def make_tail(hp, m, g):
            def author():
                for a in range(2):
                    P.op("v", lambda e, a=a: e.tensor_copy(
                        s_avt[:, 512 * a:512 * (a + 1)], av_ps[a][0:64, :]),
                        reads=[AVB[a]], writes=[("avt", a)])
                P.op("s", lambda e: e.activation(
                    s_lnt, pf[64:65, 2048:3072], AF.Ln),
                    reads=[*AVB], writes=["lnt"])
                P.op("s", lambda e: e.activation(
                    s_srec, s_lnt, AF.Exp, scale=-1.0),
                    reads=["lnt"], writes=[("srec", 0), ("srec", 1)])
                for a in range(2):
                    P.op("t", lambda e, a=a: e.matmul(
                        r_ps, s_ones64, s_srec[:, 512 * a:512 * (a + 1)],
                        start=True, stop=True),
                        reads=[("srec", a), "ones64"], writes=[RB])
                    P.op("v", lambda e, a=a: e.tensor_tensor(
                        s_asb[:, 512 * a:512 * (a + 1)],
                        s_avt[:, 512 * a:512 * (a + 1)], r_ps, ALU.mult),
                        reads=[("avt", a), RB], writes=["asb" + "AB"[a]])
                jc = 2 * g + hp
                P.op("g", lambda e, jc=jc, m=m: e.dma_start(
                    out=a_d[jc].rearrange("(h p) t -> p h t", h=2)[:, :, 512 * m:512 * (m + 1)],
                    in_=s_asb[:].rearrange("p (h t) -> p h t", h=2)),
                    reads=["asbA", "asbB"], writes=[("ad", jc, 2 * m), ("ad", jc, 2 * m + 1)],
                    dma=True)
            return author

        for hp in range(2 if attn_on else 0):
            for m in range(4):
                nkt_pair = 16 * m + 16
                if hp == 0 and m == 1 and g + 1 < n_pass:
                    author_w1_loads(g + 1)
                prevs = []
                for kt in range(nkt_pair):
                    par = ctr["sc"] % 2
                    pb = ctr["sc"] % 3
                    ctr["sc"] += 1

                    def smm(e, hp=hp, m=m, kt=kt, par=par):
                        e.matmul(sc_ps[par][:, 0:512],
                                 s_kbuf[hp][0:64, 128 * kt:128 * (kt + 1)],
                                 s_qbuf[hp][0:64, 512 * m:512 * (m + 1)],
                                 start=True, stop=True)
                        return e.matmul(sc_ps[par][:, 512:1024],
                                        s_kbuf[hp][64:128, 128 * kt:128 * (kt + 1)],
                                        s_qbuf[hp][64:128, 512 * m:512 * (m + 1)],
                                        start=True, stop=True)
                    P.op("t", smm, reads=[("kbuf", hp), ("qbuf", hp)],
                         writes=[*SCB[par]])
                    P.op("s", lambda e, par=par, pb=pb: e.activation(
                        s_p[pb], sc_ps[par], AF.Exp, scale=0.125),
                        reads=[*SCB[par]], writes=[("P", pb)])
                    if kt >= 16 * m:
                        idx = kt - 16 * m
                        P.op("v", lambda e, pb=pb, idx=idx: e.tensor_tensor(
                            s_p[pb], s_p[pb], s_mask[:, idx, :], ALU.mult),
                            reads=[("P", pb), "mask"], writes=[("P", pb)])
                    if kt == 1 and pending_tail:
                        pending_tail.pop(0)()

                    def av(e, kt, pb, hp=hp, nkt_pair=nkt_pair):
                        vv = s_vbuf[:].rearrange("p (h c) -> p h c", h=4)
                        last = None
                        for a in range(2):
                            last = e.matmul(
                                av_ps[a], vv[:, 2 * hp + a, 65 * kt:65 * kt + 65],
                                s_p[pb][:, 512 * a:512 * (a + 1)],
                                start=(kt == 0), stop=(kt == nkt_pair - 1))
                        return last
                    prevs.append((kt, pb))
                    if len(prevs) > 2:
                        pkt, ppb = prevs.pop(0)
                        P.op("t", lambda e, pkt=pkt, ppb=ppb, av=av: av(e, pkt, ppb),
                             reads=[("P", ppb), "vbuf"], writes=[*AVB])
                for pkt, ppb in prevs:
                    P.op("t", lambda e, pkt=pkt, ppb=ppb, av=av: av(e, pkt, ppb),
                         reads=[("P", ppb), "vbuf"], writes=[*AVB])
                pending_tail.append(make_tail(hp, m, g))
        if attn_on:
            while pending_tail:
                pending_tail.pop(0)()

    if stage < 5:
        P.op("y", lambda e: e.dma_start(
            out=y_out[0:128, :].rearrange("p e -> p e"),
            in_=s_x0), reads=[("x", 0)], writes=["dbg"], dma=True)
        from contextlib import ExitStack
        with ExitStack() as ctx:
            sems = {}
            for name in sorted(P.cnt):
                if not name.startswith("#"):
                    sems[name] = ctx.enter_context(nc.semaphore(name))
            block = ctx.enter_context(nc.Block())

            @block.tensor
            def _(t):
                _emit_one(P, "t", t, sems)

            @block.vector
            def _(v):
                _emit_one(P, "v", v, sems)

            @block.scalar
            def _(s):
                _emit_one(P, "s", s, sems)

            @block.gpsimd
            def _(g):
                _emit_one(P, "g", g, sems)

            @block.sync
            def _(y):
                _emit_one(P, "y", y, sems)
        return nc

    arena_res = [("kbuf", 0), ("kbuf", 1), "vbuf", ("qbuf", 0), ("qbuf", 1),
                 "w1k", "w1v", "w1q", ("lnin", 0), ("lnin", 1), ("lnow", 0),
                 ("lnow", 1), "r_t1", "r_u", "r_ush", ("P", 0), ("P", 1),
                 ("P", 2), ("avt", 0), ("avt", 1), "asbA", "asbB",
                 ("srec", 0), ("srec", 1), "lnt"]
    P.op("g", lambda e: e.dma_start(
        out=s_w2bf, in_=w2t.rearrange("(jc p) e -> p jc e", p=128)),
        writes=["w2bf", *arena_res], dma=True)
    wgr = wgt.rearrange("(dc p) h -> p dc h", p=128)
    wur = wut.rearrange("(dc p) h -> p dc h", p=128)
    wdr = wdt.rearrange("(hc p) e -> p hc e", p=128)

    def a_load(m):
        cg2, ms2 = divmod(m, 4)
        ab2 = s_a[ms2 % 2]
        ad_reads = [("ad", jc, s) for jc in range(8)
                    for s in (2 * cg2, 2 * cg2 + 1)]
        P.op("y", lambda e, m=m, ab2=ab2: e.dma_start(
            out=ab2, in_=a_d.rearrange("jc p t -> p jc t")[:, :, 128 * m:128 * (m + 1)]),
            reads=[*ad_reads, "w2bf"], writes=[("asbuf", ms2 % 2)], dma=True)

    def x_load(m):
        ms2 = m % 4
        xt2 = [s_x0, s_x1b][ms2 % 2]
        P.op("y", lambda e, m=m, xt2=xt2: e.dma_start(
            out=xt2, in_=xown_in[128 * m:128 * (m + 1), :]),
            writes=[("x", ms2 % 2)], dma=True)

    a_load(0)
    x_load(0)
    for cg in range(4):
        for ms in range(4):
            m = 4 * cg + ms
            if m + 1 < 16:
                a_load(m + 1)
                x_load(m + 1)
            ab = s_a[ms % 2]

            def w2mm(e, ab=ab):
                last = None
                for jc in range(8):
                    for eh in range(2):
                        last = e.matmul(py_ps[:, 512 * eh:512 * (eh + 1)],
                                        ab[:, jc, :],
                                        s_w2bf[:, jc, 512 * eh:512 * (eh + 1)],
                                        start=(jc == 0), stop=(jc == 7))
                return last
            P.op("t", w2mm, reads=[("asbuf", ms % 2), "w2bf"], writes=["pb0", "pb1"])
            xt = [s_x0, s_x1b][ms % 2]
            x1s = s_x1c[:, D * ms:D * (ms + 1)]
            P.op("v", lambda e, x1s=x1s, xt=xt: e.scalar_tensor_tensor(
                x1s, py_ps, 1.0, xt, ALU.mult, ALU.add),
                reads=["pb0", "pb1", ("x", ms % 2), "w2bf"], writes=[("x1", ms)])
            ln_block(x1s, ("x1", ms), "lnbf", p=ms % 2)
            transpose8(s_lnbf, "lnbf")
            P.op("v", lambda e, ms=ms: e.tensor_copy(
                s_ln2t[:, :, 128 * ms:128 * (ms + 1)],
                pbf[:].rearrange("p (dc t) -> p dc t", dc=8)),
                reads=[("tr", 0), ("tr", 1), "w2bf"], writes=[("ln2t", ms)])

        ln2_reads = [("ln2t", ms) for ms in range(4)]
        for hc in range(24):
            wb = hc % 4
            P.op("g", lambda e, hc=hc, wb=wb: e.dma_start(
                out=s_wg[wb], in_=wgr[:, :, 128 * hc:128 * (hc + 1)]),
                writes=[("wg", wb)], dma=True)
            P.op("g", lambda e, hc=hc, wb=wb: e.dma_start(
                out=s_wu[wb], in_=wur[:, :, 128 * hc:128 * (hc + 1)]),
                writes=[("wu", wb)], dma=True)

            def gmm(e, w, dst):
                last = None
                for dc in range(8):
                    last = e.matmul(dst, w[:, dc, :], s_ln2t[:, dc, :],
                                    start=(dc == 0), stop=(dc == 7))
                return last
            P.op("t", lambda e, wb=wb: gmm(e, s_wg[wb], pg_ps),
                 reads=[("wg", wb), *ln2_reads, "w2bf"], writes=["pb2"])
            P.op("t", lambda e, wb=wb: gmm(e, s_wu[wb], pu_ps),
                 reads=[("wu", wb), *ln2_reads, "w2bf"], writes=["pb3"])
            P.op("s", lambda e: e.activation(s_silu, pg_ps, AF.Silu, scale=INV127),
                 reads=["pb2", "w2bf"], writes=["silu"])
            P.op("v", lambda e, hc=hc: e.tensor_tensor(
                s_hbuf[:, 512 * hc:512 * (hc + 1)], pu_ps, s_silu, ALU.mult),
                reads=["pb3", "silu", "w2bf"], writes=[("h", hc)])

        h_reads = [("h", hc) for hc in range(24)]
        for eh in range(2):
            for hg in range(4):
                wb = hg % 2
                P.op("g", lambda e, hg=hg, eh=eh, wb=wb: e.dma_start(
                    out=s_wd[wb], in_=wdr[:, 6 * hg:6 * (hg + 1), 512 * eh:512 * (eh + 1)]),
                    writes=[("wd", wb)], dma=True)

                def dmm(e, hg=hg, wb=wb):
                    last = None
                    for hi in range(6):
                        hc = 6 * hg + hi
                        for ms in range(4):
                            last = e.matmul(
                                pd_ps[ms],
                                s_hbuf[:, 512 * hc + 128 * ms:512 * hc + 128 * (ms + 1)],
                                s_wd[wb][:, hi, :],
                                start=(hc == 0), stop=(hc == 23))
                    return last
                P.op("t", dmm, reads=[("wd", wb), *h_reads, "w2bf"],
                     writes=["pb2", "pb3", "pb4", "pb5"])
            for ms in range(4):
                P.op("v", lambda e, ms=ms, eh=eh: e.scalar_tensor_tensor(
                    s_ysb[:, D * ms + 512 * eh: D * ms + 512 * (eh + 1)],
                    pd_ps[ms], DOWN_SCALE,
                    s_x1c[:, D * ms + 512 * eh: D * ms + 512 * (eh + 1)],
                    ALU.mult, ALU.add),
                    reads=["pb" + str(2 + ms), ("x1", ms), "w2bf"], writes=[("ysb", ms, eh)])
        for ms in range(4):
            P.op("y", lambda e, cg=cg, ms=ms: e.dma_start(
                out=y_out[512 * cg + 128 * ms: 512 * cg + 128 * (ms + 1), :],
                in_=s_ysb[:, D * ms:D * (ms + 1)]),
                reads=[("ysb", ms, 0), ("ysb", ms, 1), "w2bf"],
                writes=[("yout", cg, ms)], dma=True)

    from contextlib import ExitStack
    with ExitStack() as ctx:
        sems = {}
        for name in sorted(P.cnt):
            if not name.startswith("#"):
                sems[name] = ctx.enter_context(nc.semaphore(name))
        block = ctx.enter_context(nc.Block())

        @block.tensor
        def _(t):
            _emit_one(P, "t", t, sems)

        @block.vector
        def _(v):
            _emit_one(P, "v", v, sems)

        @block.scalar
        def _(s):
            _emit_one(P, "s", s, sems)

        @block.gpsimd
        def _(g):
            _emit_one(P, "g", g, sems)

        @block.sync
        def _(y):
            _emit_one(P, "y", y, sems)

    return nc




def _tables(core):
    chunks = own_chunk_positions(core)
    hs = HD
    inv_freq = (1.0 / (10000.0 ** (np.arange(0, hs, 2, dtype=np.float32) / hs))
                ).astype(np.float32)
    own_tok = np.concatenate([np.arange(256 * p, 256 * (p + 1)) for p in chunks])
    pos = np.concatenate([np.arange(T), own_tok]).astype(np.float32)
    ang = (pos[:, None] * inv_freq[None, :]).astype(np.float32)
    cos = np.cos(ang).astype(np.float32)
    sin = np.sin(ang).astype(np.float32)
    rope = np.zeros((128, T + 2048), dtype=np.float32)
    for p in range(64):
        rope[p] = cos[:, p % 32]
    for p in range(64, 128):
        d = p - 64
        s = sin[:, d % 32]
        rope[p] = s if d < 32 else -s

    q = core % 4
    off_e, off_o = q, 3 - q
    masks = np.zeros((16, 128, 1024), dtype=np.float32)
    k_idx = np.arange(128)[:, None]
    c_idx = np.arange(256)[None, :]
    for r in range(16):
        pat_e = (128 * r + k_idx <= 256 * off_e + c_idx).astype(np.float32)
        pat_o = (128 * r + k_idx <= 1024 + 256 * off_o + c_idx).astype(np.float32)
        half = np.concatenate([pat_e, pat_o], axis=1)
        masks[r, :, 0:512] = half
        masks[r, :, 512:1024] = half
    return rope.astype(ml_dtypes.bfloat16), masks.astype(ml_dtypes.bfloat16), own_tok


def _host(a, dt):
    try:
        import jax
        a = jax.device_get(a)
    except Exception:
        pass
    return np.asarray(a).astype(dt, copy=False)


def kernel(L0_attn_x, L0_attn_W1, L0_attn_W2, L0_ffn_w_gate, L0_ffn_w_up,
           L0_ffn_w_down):
    x = np.ascontiguousarray(_host(L0_attn_x, np.float32))
    w1t = np.ascontiguousarray(_host(L0_attn_W1, np.float32).T)
    w2t = np.ascontiguousarray(_host(L0_attn_W2, np.float32).T)
    wgt = np.ascontiguousarray(_host(L0_ffn_w_gate, np.int32).T)
    wut = np.ascontiguousarray(_host(L0_ffn_w_up, np.int32).T)
    wdt = np.ascontiguousarray(_host(L0_ffn_w_down, np.int32).T)
    ident = np.eye(128).astype(ml_dtypes.bfloat16)
    ones64 = np.ones((1, 64), dtype=ml_dtypes.bfloat16)

    nc = build_program()
    in_maps = []
    own_toks = []
    for c in range(8):
        b = c // 4
        rope, masks, own_tok = _tables(c)
        own_toks.append(own_tok)
        in_maps.append(dict(
            x=np.ascontiguousarray(x[b]),
            xown=np.ascontiguousarray(x[b][own_tok]),
            w1t=w1t, w2t=w2t, wgt=wgt, wut=wut, wdt=wdt,
            ropetab=rope, maskst=masks, ident=ident, ones64=ones64))
    res = run_bass_kernel_spmd(nc, in_maps, core_ids=list(range(8)))

    out = np.empty((B, T, D), dtype=np.float32)
    for c in range(8):
        b = c // 4
        out[b, own_toks[c]] = res.results[c]["y"]
    return out


# version 25 (final)
# speedup vs baseline: 1.1983x; 1.1983x over previous
import numpy as np
import ml_dtypes

import concourse.bass as bass
import concourse.mybir as mybir
from concourse.bass_utils import run_bass_kernel_spmd

F32 = mybir.dt.float32
BF16 = mybir.dt.bfloat16
I32 = mybir.dt.int32
AF = mybir.ActivationFunctionType
ALU = mybir.AluOpType
AX = mybir.AxisListType

B, T, D = 2, 8192, 1024
NH, HD = 16, 64
HID = 3072
EPS = 1e-5
INV127 = float(np.float32(1.0 / 127.0))
DOWN_SCALE = INV127 * INV127
NKT = T // 128
NSLOT = 8


def own_chunk_positions(core):
    q = core % 4
    return [4 * s + (q if s % 2 == 0 else 3 - q) for s in range(NSLOT)]



class Prog:

    ENGS = ("t", "v", "s", "g", "y")

    def __init__(self, serial=False):
        self.stream = {e: [] for e in self.ENGS}
        self.cnt = {}
        self.writer = {}
        self.readers = {}
        self.serial = serial
        self.last_ev = None

    NROT = 16

    def op(self, eng, fn, reads=(), writes=(), dma=False, slow=False):
        if dma:
            idx = self.cnt.get("#d" + eng, 0)
            self.cnt["#d" + eng] = idx + 1
            sem = "d%s%02d" % (eng, idx % self.NROT)
            self.cnt[sem] = self.cnt.get(sem, 0) + 16
        else:
            sem = "c" + eng
            self.cnt[sem] = self.cnt.get(sem, 0) + 1
        ev = (eng, sem, self.cnt[sem], dma or slow)
        deps = []

        def need(p):
            if p is None:
                return
            p_eng, p_sem, p_val, p_unsafe = p
            if p_eng == eng == "t" and not p_unsafe and not dma:
                return
            deps.append((p_sem, p_val))

        if self.serial:
            need(self.last_ev)
        if dma and self.cnt[sem] > 16:
            deps.append((sem, self.cnt[sem] - 16))
        for r in reads:
            need(self.writer.get(r))
        for w in writes:
            need(self.writer.get(w))
            for rd in self.readers.get(w, ()):
                need(rd)
        for w in writes:
            self.writer[w] = ev
            self.readers[w] = []
        for r in reads:
            self.readers.setdefault(r, []).append(ev)
        self.stream[eng].append((fn, deps, sem, self.cnt[sem], dma))
        self.last_ev = ev
        return ev


def _emit_one(P, e, eng, sems):
    waited = {}
    for fn, deps, sem, val, dma in P.stream[e]:
        merged = {}
        for s, v in deps:
            if waited.get(s, 0) < v:
                merged[s] = max(merged.get(s, 0), v)
        for s, v in sorted(merged.items()):
            eng.wait_ge(sems[s], v)
            waited[s] = v
        inst = fn(eng)
        inst.then_inc(sems[sem], 16 if dma else 1)
    for s in sorted(P.cnt):
        if s.startswith("d" + e) and not s.startswith("#"):
            eng.wait_ge(sems[s], P.cnt[s])




def build_program():
    import os
    stage = int(os.environ.get("KSTAGE", "5"))
    serial = os.environ.get("KSERIAL", "0") == "1"
    nc = bass.Bass("TRN2", target_bir_lowering=False)

    x_in = nc.dram_tensor("x", [T, D], F32, kind="ExternalInput").ap()
    xown_in = nc.dram_tensor("xown", [2048, D], F32, kind="ExternalInput").ap()
    w1t = nc.dram_tensor("w1t", [D, 3 * D], F32, kind="ExternalInput").ap()
    w2t = nc.dram_tensor("w2t", [D, D], F32, kind="ExternalInput").ap()
    wgt = nc.dram_tensor("wgt", [D, HID], I32, kind="ExternalInput").ap()
    wut = nc.dram_tensor("wut", [D, HID], I32, kind="ExternalInput").ap()
    wdt = nc.dram_tensor("wdt", [HID, D], I32, kind="ExternalInput").ap()
    ropetab = nc.dram_tensor("ropetab", [128, T + 2048], BF16, kind="ExternalInput").ap()
    maskst = nc.dram_tensor("maskst", [16, 128, 1024], BF16,
                            kind="ExternalInput").ap()
    ident_in = nc.dram_tensor("ident", [128, 128], BF16, kind="ExternalInput").ap()
    ones64_in = nc.dram_tensor("ones64", [1, 64], BF16, kind="ExternalInput").ap()
    y_out = nc.dram_tensor("y", [2048, D], F32, kind="ExternalOutput").ap()

    lnxt_d = nc.dram_tensor("lnxt_d", [8, 128, T], BF16, kind="Internal").ap()
    lnown_d = nc.dram_tensor("lnown_d", [8, 128, 2048], BF16, kind="Internal").ap()
    a_d = nc.dram_tensor("a_d", [8, 128, 2048], BF16, kind="Internal").ap()

    cur = [nc.sbuf_base + 31]
    cur[0] -= cur[0] % 32

    sb_at = {}

    def sb(name, shape, dtype, at=None):
        nbytes = int(np.prod(shape[1:])) * mybir.dt.size(dtype)
        if at is None:
            at = cur[0]
            cur[0] = at + ((nbytes + 31) // 32) * 32
        sb_at[name] = at
        return nc.alloc_sbuf_tensor_at(name, list(shape), dtype, offset=at).ap()

    s_rope = sb("s_rope", [128, T + 2048], BF16)
    s_mask = sb("s_mask", [128, 16, 1024], BF16)
    s_ident = sb("s_ident", [128, 128], BF16)
    s_ones64 = sb("s_ones64", [1, 64], BF16)
    s_stats = sb("s_stats", [128, 32], F32)
    s_lnt = sb("s_lnt", [1, 1024], F32)
    s_x0 = sb("s_x0", [128, D], F32)
    s_x1b = sb("s_x1b", [128, D], F32)
    s_sq = sb("s_sq", [128, D], BF16)
    s_lnbf = sb("s_lnbf", [128, D], BF16)
    s_lnbf2 = sb("s_lnbf2", [128, D], BF16)

    arena = cur[0]
    s_kbuf = [sb("s_kbuf0", [128, T], BF16), sb("s_kbuf1", [128, T], BF16)]
    s_vbuf = sb("s_vbuf", [128, 4 * NKT * 65], BF16)
    s_qbuf = [sb("s_qbuf0", [128, 2048], BF16), sb("s_qbuf1", [128, 2048], BF16)]
    s_w1bf = sb("s_w1bf", [128, 8, 768], BF16)
    s_lnin = sb("s_lnin", [128, 8, 1024], BF16)
    s_lnow = sb("s_lnow", [128, 8, 512], BF16)
    s_stage = sb("s_stage", [128, 8, 512], BF16)
    s_x2 = sb("s_x2", [128, D], F32)
    s_x3 = sb("s_x3", [128, D], F32)
    s_t1 = sb("s_t1", [128, 1024], BF16)
    s_u = sb("s_u", [128, 1024], BF16)
    s_ush = sb("s_ush", [128, 1024], BF16)
    s_p = [sb("s_p%d" % i, [128, 1024], BF16) for i in range(3)]
    s_avt = sb("s_avt", [64, 1024], F32, at=sb_at["s_stage"])
    s_srec = sb("s_srec", [1, 1024], BF16, at=sb_at["s_stage"] + 4096)
    s_asb = sb("s_asb", [64, 1024], BF16, at=sb_at["s_stage"] + 6144)
    ph12_end = cur[0]

    cur[0] = arena
    s_a = [sb("s_a0", [128, 8, 128], BF16), sb("s_a1", [128, 8, 128], BF16)]
    s_w2bf = sb("s_w2bf", [128, 8, 1024], BF16)
    s_x1c = sb("s_x1c", [128, 4 * D], F32)
    s_ln2t = sb("s_ln2t", [128, 8, 512], BF16)
    s_wg = [sb("s_wg%d" % i, [128, 8, 128], BF16) for i in range(4)]
    s_wu = [sb("s_wu%d" % i, [128, 8, 128], BF16) for i in range(4)]
    s_wd = [sb("s_wd0", [128, 6, 512], BF16), sb("s_wd1", [128, 6, 512], BF16)]
    s_hbuf = sb("s_hbuf", [128, 24 * 512], BF16)
    s_silu = sb("s_silu", [128, 512], BF16)
    s_ysb = sb("s_ysb", [128, 4 * D], F32)
    ph3_end = cur[0]
    assert max(ph12_end, ph3_end) <= nc.sbuf_top - 32, (ph12_end, ph3_end, nc.sbuf_top)

    pf = nc.alloc_psum_tensor("pf", [128, 3584], F32).ap()
    pbf = nc.alloc_psum_tensor("pbf", [128, 1024], BF16).ap()

    kq_ps = [pf[:, 0:1024], pf[:, 1024:2048]]
    v_ps = [pf[:, 2048:2304], pf[:, 2560:2816], pf[:, 3072:3328]]
    sc_ps = [pf[:, 0:1024], pf[:, 1024:2048]]
    av_ps = [pf[0:65, 2048:2560], pf[0:65, 2560:3072]]
    r_ps = pf[0:64, 3072:3584]
    py_ps = pf[:, 0:1024]
    pg_ps = pf[:, 1024:1536]
    pu_ps = pf[:, 1536:2048]
    pd_ps = [pf[:, 1024 + 512 * i: 1536 + 512 * i] for i in range(4)]

    KQB = [("pb0", "pb1"), ("pb2", "pb3")]
    VB = ["pb4", "pb5", "pb6"]
    SCB = [("pb0", "pb1"), ("pb2", "pb3")]
    AVB = ("pb4", "pb5")
    RB = "pb6"
    TRB = "pb7"

    P = Prog(serial=serial)

    P.op("y", lambda e: e.dma_start(out=s_rope, in_=ropetab), writes=["rope"], dma=True)
    P.op("y", lambda e: e.dma_start(
        out=s_mask, in_=maskst.rearrange("r p x -> p r x")), writes=["mask"], dma=True)
    P.op("y", lambda e: e.dma_start(out=s_ident, in_=ident_in), writes=["ident"], dma=True)
    P.op("y", lambda e: e.dma_start(out=s_ones64, in_=ones64_in), writes=["ones64"], dma=True)

    def _stat(p, k):
        i = ["sumx", "sumsq", "m", "var", "msq", "rstd", "lnv", "negmr"].index(k)
        return s_stats[:, 8 * p + i:8 * p + i + 1]

    def ln_front(x_ap, xres, p):
        st = lambda k: _stat(p, k)
        rn = lambda k: ("st_" + k, p)
        P.op("v", lambda e: e.reduce_sum(st("sumx"), x_ap, AX.X),
             reads=[xres], writes=[rn("sumx")], slow=True)
        P.op("s", lambda e: e.activation(s_sq, x_ap, AF.Square, accum_out=st("sumsq")),
             reads=[xres], writes=[rn("sumsq"), "sq"], slow=True)

    def ln_back(x_ap, xres, out_ap, out_res, p):
        st = lambda k: _stat(p, k)
        rn = lambda k: ("st_" + k, p)
        P.op("v", lambda e: e.tensor_scalar_mul(st("m"), st("sumx"), 1.0 / D),
             reads=[rn("sumx")], writes=[rn("m")])
        P.op("v", lambda e: e.tensor_tensor(st("msq"), st("m"), st("m"), ALU.mult),
             reads=[rn("m")], writes=[rn("msq")])
        P.op("v", lambda e: e.tensor_scalar(st("var"), st("sumsq"), 1.0 / D, EPS,
                                            ALU.mult, ALU.add),
             reads=[rn("sumsq")], writes=[rn("var")])
        P.op("v", lambda e: e.tensor_tensor(st("var"), st("var"), st("msq"),
                                            ALU.subtract),
             reads=[rn("var"), rn("msq")], writes=[rn("var")])

        P.op("s", lambda e: e.activation(st("lnv"), st("var"), AF.Ln),
             reads=[rn("var")], writes=[rn("lnv")])
        P.op("s", lambda e: e.activation(st("rstd"), st("lnv"), AF.Exp, scale=-0.5),
             reads=[rn("lnv")], writes=[rn("rstd")])
        P.op("v", lambda e: e.tensor_scalar(st("negmr"), st("m"), st("rstd"), -1.0,
                                            ALU.mult, ALU.mult),
             reads=[rn("m"), rn("rstd")], writes=[rn("negmr")])
        P.op("s", lambda e: e.activation(out_ap, x_ap, AF.Identity,
                                         bias=st("negmr"), scale=st("rstd")),
             reads=[xres, rn("negmr"), rn("rstd")], writes=[out_res])

    def ln_block(x_ap, xres, out_res, p=0):
        ln_front(x_ap, xres, p)
        ln_back(x_ap, xres, s_lnbf, out_res, p)

    def transpose_half(in_ap, in_res, h):
        def f(e, h=h):
            last = None
            for c in range(4 * h, 4 * h + 4):
                last = e.transpose(pbf[:, 128 * c:128 * (c + 1)],
                                   in_ap[:, 128 * c:128 * (c + 1)], s_ident)
            return last
        P.op("t", f, reads=[in_res, "ident"], writes=[("tr", h)])

    def transpose8(in_ap, in_res):
        def f(e):
            last = None
            for c in range(8):
                last = e.transpose(pbf[:, 128 * c:128 * (c + 1)],
                                   in_ap[:, 128 * c:128 * (c + 1)], s_ident)
            return last
        P.op("t", f, reads=[in_res, "ident"], writes=[("tr", 0), ("tr", 1)])

    ph1_pending = []
    ph1_copy_q = []
    rope_q = []
    xbufs = [s_x0, s_x1b, s_x2, s_x3]

    def ph1_finish(info):
        i, xt, src_dram, dst_dram, res_prefix, feed = info
        b2 = i % 2
        lnb = [s_lnbf, s_lnbf2][b2]
        ln_back(xt, ("x", i % 4), lnb, ("lnbf", b2), p=i % 4)
        slot = i % 4
        grp = i // 4
        lb = grp % 2
        transpose8(lnb, ("lnbf", b2))

        def do_copy(slot=slot, grp=grp, lb=lb, feed=feed, dst_dram=dst_dram,
                    res_prefix=res_prefix):
            if feed:
                P.op("v", lambda e: e.tensor_copy(
                    s_lnin[:, :, 512 * lb + 128 * slot:512 * lb + 128 * (slot + 1)],
                    pbf[:].rearrange("p (dc t) -> p dc t", dc=8)),
                    reads=[("tr", 0), ("tr", 1)], writes=[("lnin", lb)])
                if slot == 3:
                    P.op("y", lambda e: e.dma_start(
                        out=dst_dram.rearrange("dc p t -> p dc t")[:, :, 512 * grp:512 * (grp + 1)],
                        in_=s_lnin[:, :, 512 * lb:512 * (lb + 1)]),
                        reads=[("lnin", lb)],
                        writes=[(res_prefix, grp)], dma=True)
            else:
                P.op("v", lambda e: e.tensor_copy(
                    s_stage[:, :, 128 * slot:128 * (slot + 1)],
                    pbf[:].rearrange("p (dc t) -> p dc t", dc=8)),
                    reads=[("tr", 0), ("tr", 1)], writes=[("stage", slot)])
                if slot == 3:
                    P.op("y", lambda e: e.dma_start(
                        out=dst_dram.rearrange("dc p t -> p dc t")[:, :, 512 * grp:512 * (grp + 1)],
                        in_=s_stage),
                        reads=[("stage", 0), ("stage", 1), ("stage", 2), ("stage", 3)],
                        writes=[(res_prefix, grp)], dma=True)
        ph1_copy_q.append(do_copy)

    def ln_transpose_store_range(lo, hi, src_dram, dst_dram, res_prefix, feed=False):
        def dma_x(i):
            xt = xbufs[i % 4]
            P.op("y", lambda e, i=i, xt=xt: e.dma_start(
                out=xt, in_=src_dram[128 * i:128 * (i + 1), :]),
                writes=[("x", i % 4)], dma=True)
        for i in range(lo, min(lo + 2, hi)):
            dma_x(i)
        for i in range(lo, hi):
            if len(ph1_pending) > 1:
                if ph1_copy_q:
                    ph1_copy_q.pop(0)()
                ph1_finish(ph1_pending.pop(0))
            if i + 2 < hi:
                dma_x(i + 2)
            ln_front(xbufs[i % 4], ("x", i % 4), p=i % 4)
            if feed and rope_q:
                rope_q.pop(0)()
            ph1_pending.append((i, xbufs[i % 4], src_dram, dst_dram,
                                res_prefix, feed))
        while ph1_pending:
            if ph1_copy_q:
                ph1_copy_q.pop(0)()
            ph1_finish(ph1_pending.pop(0))
        while ph1_copy_q:
            ph1_copy_q.pop(0)()


    debug_finish = [None]

    w1r = w1t.rearrange("(dc p) j -> p dc j", p=128)
    ctr = {"kq": 0, "v": 0, "sc": 0}

    def rope_store(dst_ap, dst_res, ps_ap, ps_res, c0, width, defer=None):
        cos = s_rope[0:64, c0:c0 + width]
        sin = s_rope[64:128, c0:c0 + width]
        t1 = s_t1[:, 0:width]
        u = s_u[:, 0:width]
        ush = s_ush[:, 0:width]

        def f1(e):
            e.tensor_tensor(t1[0:64, :], ps_ap[0:64, :], cos, ALU.mult)
            return e.tensor_tensor(t1[64:128, :], ps_ap[64:128, :], cos, ALU.mult)

        def f2(e):
            e.tensor_tensor(u[0:64, :], ps_ap[0:64, :], sin, ALU.mult)
            return e.tensor_tensor(u[64:128, :], ps_ap[64:128, :], sin, ALU.mult)

        def f3(e):
            e.tensor_copy(ush[0:32, :], u[32:64, :])
            e.tensor_copy(ush[32:64, :], u[0:32, :])
            e.tensor_copy(ush[64:96, :], u[96:128, :])
            return e.tensor_copy(ush[96:128, :], u[64:96, :])
        thunks = [
            lambda: P.op("v", f1, reads=[*ps_res, "rope"], writes=["r_t1"]),
            lambda: P.op("v", f2, reads=[*ps_res, "rope"], writes=["r_u"]),
            lambda: P.op("v", f3, reads=["r_u"], writes=["r_ush"]),
            lambda: P.op("v", lambda e: e.tensor_tensor(dst_ap, t1, ush, ALU.add),
                         reads=["r_t1", "r_ush"], writes=[dst_res]),
        ]
        if defer is None:
            for th in thunks:
                th()
        else:
            defer.extend(thunks)

    n_pass = 0 if stage < 2 else (1 if stage < 4 else 4)
    attn_on = stage >= 3
    katt = int(os.environ.get("KATT", "5"))
    kslots = int(os.environ.get("KSLOTS", "99"))
    kquarts = int(os.environ.get("KQUARTS", "999999"))

    def author_w1_loads(g):
        kcol, vcol, qcol = D + 256 * g, 2 * D + 256 * g, 256 * g
        P.op("g", lambda e, c=kcol: e.dma_start(
            out=s_w1bf[:, :, 0:256], in_=w1r[:, :, c:c + 256]),
            writes=["w1k"], dma=True)
        P.op("g", lambda e, c=vcol: e.dma_start(
            out=s_w1bf[:, :, 256:512], in_=w1r[:, :, c:c + 256]),
            writes=["w1v"], dma=True)
        P.op("g", lambda e, c=qcol: e.dma_start(
            out=s_w1bf[:, :, 512:768], in_=w1r[:, :, c:c + 256]),
            writes=["w1q"], dma=True)

    P.op("g", lambda e: e.memset(s_vbuf, 1.0), writes=["vbuf"], slow=True)
    author_w1_loads(0)

    def author_qkv_jm(g, jm, resident=False):
        if resident:
            while rope_q:
                rope_q.pop(0)()
        if not resident:
            for half in range(2):
                j = 2 * jm + half
                lb = j % 2
                P.op("y", lambda e, j=j, lb=lb: e.dma_start(
                    out=s_lnin[:, :, 512 * lb:512 * (lb + 1)],
                    in_=lnxt_d.rearrange("dc p t -> p dc t")[:, :, 512 * j:512 * (j + 1)]),
                    reads=[("lnxt", j)], writes=[("lnin", lb)], dma=True)
        if True:
            for half in range(2):
                for ts_ in range(4):
                    vpar = ctr["v"] % 3
                    ctr["v"] += 1
                    kt = 8 * jm + 4 * half + ts_

                    def vmm(e, half=half, ts_=ts_, vpar=vpar):
                        last = None
                        for dc in range(8):
                            last = e.matmul(
                                v_ps[vpar],
                                s_lnin[:, dc, 512 * half + 128 * ts_:
                                       512 * half + 128 * (ts_ + 1)],
                                s_w1bf[:, dc, 256:512],
                                start=(dc == 0), stop=(dc == 7))
                        return last
                    P.op("t", vmm, reads=[("lnin", half), "w1v"], writes=[VB[vpar]])
                    P.op("s", lambda e, vpar=vpar, kt=kt: e.copy(
                        s_vbuf[:].rearrange("p (h k d) -> p h k d", h=4, k=NKT, d=65)[:, :, kt, 0:64],
                        v_ps[vpar][:].rearrange("p (h d) -> p h d", h=4)),
                        reads=[VB[vpar]], writes=["vbuf"])
            kq_pars = []
            for hp in range(2):
                par = ctr["kq"] % 2
                ctr["kq"] += 1
                kq_pars.append(par)

                def kmm(e, hp=hp, par=par):
                    last = None
                    for dc in range(8):
                        for half in range(2):
                            last = e.matmul(
                                kq_ps[par][:, 512 * half:512 * (half + 1)],
                                s_w1bf[:, dc, hp * 128:(hp + 1) * 128],
                                s_lnin[:, dc, 512 * half:512 * (half + 1)],
                                start=(dc == 0), stop=(dc == 7))
                    return last
                P.op("t", kmm, reads=[("lnin", 0), ("lnin", 1), "w1k"],
                     writes=[*KQB[par]])
            for hp in range(2):
                rope_store(s_kbuf[hp][:, 1024 * jm:1024 * (jm + 1)], ("kbuf", hp),
                           kq_ps[kq_pars[hp]], KQB[kq_pars[hp]], 1024 * jm, 1024,
                           defer=(rope_q if resident else None))
    def author_qblock_sm(g, sm, resident=False):
        if resident:
            while rope_q:
                rope_q.pop(0)()
        if not resident:
            for half in range(2):
                ss = 2 * sm + half
                P.op("y", lambda e, ss=ss, half=half: e.dma_start(
                    out=s_lnow[:, :, 256 * half:256 * (half + 1)],
                    in_=lnown_d.rearrange("dc p t -> p dc t")[:, :, 256 * ss:256 * (ss + 1)]),
                    reads=[("lnown", ss // 2)], writes=[("lnow", half)], dma=True)
        lb = sm % 2
        for hp in range(2):
            par = ctr["kq"] % 2
            ctr["kq"] += 1

            def qmm(e, hp=hp, par=par, resident=resident, lb=lb):
                last = None
                for dc in range(8):
                    rhs = (s_lnin[:, dc, 512 * lb:512 * (lb + 1)]
                           if resident else s_lnow[:, dc, :])
                    last = e.matmul(
                        kq_ps[par][:, 0:512],
                        s_w1bf[:, dc, 512 + hp * 128:512 + (hp + 1) * 128],
                        rhs,
                        start=(dc == 0), stop=(dc == 7))
                return last
            P.op("t", qmm,
                 reads=([("lnin", lb)] if resident else [("lnow", 0), ("lnow", 1)]) + ["w1q"],
                 writes=[*KQB[par]])
            rope_store(s_qbuf[hp][:, 512 * sm:512 * (sm + 1)], ("qbuf", hp),
                       kq_ps[par][:, 0:512], KQB[par], T + 512 * sm, 512,
                       defer=(rope_q if resident else None))

    def author_qblock(g):
        for sm in range(4):
            author_qblock_sm(g, sm)

    feed = n_pass > 0
    for jm in range(8):
        ln_transpose_store_range(8 * jm, 8 * jm + 8, x_in, lnxt_d, "lnxt", feed=feed)
        if n_pass > 0:
            author_qkv_jm(0, jm, resident=True)
    if n_pass > 0:
        for sm in range(4):
            ln_transpose_store_range(4 * sm, 4 * sm + 4, xown_in, lnown_d, "lnown",
                                     feed=True)
            author_qblock_sm(0, sm, resident=True)
        while rope_q:
            rope_q.pop(0)()
    else:
        ln_transpose_store_range(0, 16, xown_in, lnown_d, "lnown")

    for g in range(n_pass):
        if g > 0:
            for jm in range(8):
                author_qkv_jm(g, jm)
            author_qblock(g)

        pending_tail = []

        def make_tail(hp, m, g):
            def author():
                for a in range(2):
                    P.op("v", lambda e, a=a: e.tensor_copy(
                        s_avt[:, 512 * a:512 * (a + 1)], av_ps[a][0:64, :]),
                        reads=[AVB[a]], writes=[("avt", a)])
                P.op("s", lambda e: e.activation(
                    s_lnt, pf[64:65, 2048:3072], AF.Ln),
                    reads=[*AVB], writes=["lnt"])
                P.op("s", lambda e: e.activation(
                    s_srec, s_lnt, AF.Exp, scale=-1.0),
                    reads=["lnt"], writes=[("srec", 0), ("srec", 1)])
                for a in range(2):
                    P.op("t", lambda e, a=a: e.matmul(
                        r_ps, s_ones64, s_srec[:, 512 * a:512 * (a + 1)],
                        start=True, stop=True),
                        reads=[("srec", a), "ones64"], writes=[RB])
                    P.op("v", lambda e, a=a: e.tensor_tensor(
                        s_asb[:, 512 * a:512 * (a + 1)],
                        s_avt[:, 512 * a:512 * (a + 1)], r_ps, ALU.mult),
                        reads=[("avt", a), RB], writes=["asb" + "AB"[a]])
                jc = 2 * g + hp
                P.op("g", lambda e, jc=jc, m=m: e.dma_start(
                    out=a_d[jc].rearrange("(h p) t -> p h t", h=2)[:, :, 512 * m:512 * (m + 1)],
                    in_=s_asb[:].rearrange("p (h t) -> p h t", h=2)),
                    reads=["asbA", "asbB"], writes=[("ad", jc, 2 * m), ("ad", jc, 2 * m + 1)],
                    dma=True)
            return author

        for hp in range(2 if attn_on else 0):
            for m in range(4):
                nkt_pair = 16 * m + 16
                if hp == 0 and m == 1 and g + 1 < n_pass:
                    author_w1_loads(g + 1)
                prevs = []
                for kt in range(nkt_pair):
                    par = ctr["sc"] % 2
                    pb = ctr["sc"] % 3
                    ctr["sc"] += 1

                    def smm(e, hp=hp, m=m, kt=kt, par=par):
                        e.matmul(sc_ps[par][:, 0:512],
                                 s_kbuf[hp][0:64, 128 * kt:128 * (kt + 1)],
                                 s_qbuf[hp][0:64, 512 * m:512 * (m + 1)],
                                 start=True, stop=True)
                        return e.matmul(sc_ps[par][:, 512:1024],
                                        s_kbuf[hp][64:128, 128 * kt:128 * (kt + 1)],
                                        s_qbuf[hp][64:128, 512 * m:512 * (m + 1)],
                                        start=True, stop=True)
                    P.op("t", smm, reads=[("kbuf", hp), ("qbuf", hp)],
                         writes=[*SCB[par]])
                    P.op("s", lambda e, par=par, pb=pb: e.activation(
                        s_p[pb], sc_ps[par], AF.Exp, scale=0.125),
                        reads=[*SCB[par]], writes=[("P", pb)])
                    if kt >= 16 * m:
                        idx = kt - 16 * m
                        P.op("v", lambda e, pb=pb, idx=idx: e.tensor_tensor(
                            s_p[pb], s_p[pb], s_mask[:, idx, :], ALU.mult),
                            reads=[("P", pb), "mask"], writes=[("P", pb)])
                    if kt == 1 and pending_tail:
                        pending_tail.pop(0)()

                    def av(e, kt, pb, hp=hp, nkt_pair=nkt_pair):
                        vv = s_vbuf[:].rearrange("p (h c) -> p h c", h=4)
                        last = None
                        for a in range(2):
                            last = e.matmul(
                                av_ps[a], vv[:, 2 * hp + a, 65 * kt:65 * kt + 65],
                                s_p[pb][:, 512 * a:512 * (a + 1)],
                                start=(kt == 0), stop=(kt == nkt_pair - 1))
                        return last
                    prevs.append((kt, pb))
                    if len(prevs) > 2:
                        pkt, ppb = prevs.pop(0)
                        P.op("t", lambda e, pkt=pkt, ppb=ppb, av=av: av(e, pkt, ppb),
                             reads=[("P", ppb), "vbuf"], writes=[*AVB])
                for pkt, ppb in prevs:
                    P.op("t", lambda e, pkt=pkt, ppb=ppb, av=av: av(e, pkt, ppb),
                         reads=[("P", ppb), "vbuf"], writes=[*AVB])
                pending_tail.append(make_tail(hp, m, g))
        if attn_on:
            while pending_tail:
                pending_tail.pop(0)()

    if stage < 5:
        P.op("y", lambda e: e.dma_start(
            out=y_out[0:128, :].rearrange("p e -> p e"),
            in_=s_x0), reads=[("x", 0)], writes=["dbg"], dma=True)
        from contextlib import ExitStack
        with ExitStack() as ctx:
            sems = {}
            for name in sorted(P.cnt):
                if not name.startswith("#"):
                    sems[name] = ctx.enter_context(nc.semaphore(name))
            block = ctx.enter_context(nc.Block())

            @block.tensor
            def _(t):
                _emit_one(P, "t", t, sems)

            @block.vector
            def _(v):
                _emit_one(P, "v", v, sems)

            @block.scalar
            def _(s):
                _emit_one(P, "s", s, sems)

            @block.gpsimd
            def _(g):
                _emit_one(P, "g", g, sems)

            @block.sync
            def _(y):
                _emit_one(P, "y", y, sems)
        return nc

    arena_res = [("kbuf", 0), ("kbuf", 1), "vbuf", ("qbuf", 0), ("qbuf", 1),
                 "w1k", "w1v", "w1q", ("lnin", 0), ("lnin", 1), ("lnow", 0),
                 ("lnow", 1), "r_t1", "r_u", "r_ush", ("P", 0), ("P", 1),
                 ("P", 2), ("avt", 0), ("avt", 1), "asbA", "asbB",
                 ("srec", 0), ("srec", 1), "lnt"]
    P.op("g", lambda e: e.dma_start(
        out=s_w2bf, in_=w2t.rearrange("(jc p) e -> p jc e", p=128)),
        writes=["w2bf", *arena_res], dma=True)
    wgr = wgt.rearrange("(dc p) h -> p dc h", p=128)
    wur = wut.rearrange("(dc p) h -> p dc h", p=128)
    wdr = wdt.rearrange("(hc p) e -> p hc e", p=128)

    def a_load(m):
        cg2, ms2 = divmod(m, 4)
        ab2 = s_a[ms2 % 2]
        ad_reads = [("ad", jc, s) for jc in range(8)
                    for s in (2 * cg2, 2 * cg2 + 1)]
        P.op("y", lambda e, m=m, ab2=ab2: e.dma_start(
            out=ab2, in_=a_d.rearrange("jc p t -> p jc t")[:, :, 128 * m:128 * (m + 1)]),
            reads=[*ad_reads, "w2bf"], writes=[("asbuf", ms2 % 2)], dma=True)

    def x_load(m):
        ms2 = m % 4
        xt2 = [s_x0, s_x1b][ms2 % 2]
        P.op("y", lambda e, m=m, xt2=xt2: e.dma_start(
            out=xt2, in_=xown_in[128 * m:128 * (m + 1), :]),
            writes=[("x", ms2 % 2)], dma=True)

    a_load(0)
    x_load(0)
    for cg in range(4):
        for ms in range(4):
            m = 4 * cg + ms
            if m + 1 < 16:
                a_load(m + 1)
                x_load(m + 1)
            ab = s_a[ms % 2]

            def w2mm(e, ab=ab):
                last = None
                for jc in range(8):
                    for eh in range(2):
                        last = e.matmul(py_ps[:, 512 * eh:512 * (eh + 1)],
                                        ab[:, jc, :],
                                        s_w2bf[:, jc, 512 * eh:512 * (eh + 1)],
                                        start=(jc == 0), stop=(jc == 7))
                return last
            P.op("t", w2mm, reads=[("asbuf", ms % 2), "w2bf"], writes=["pb0", "pb1"])
            xt = [s_x0, s_x1b][ms % 2]
            x1s = s_x1c[:, D * ms:D * (ms + 1)]
            P.op("v", lambda e, x1s=x1s, xt=xt: e.scalar_tensor_tensor(
                x1s, py_ps, 1.0, xt, ALU.mult, ALU.add),
                reads=["pb0", "pb1", ("x", ms % 2), "w2bf"], writes=[("x1", ms)])
            ln_block(x1s, ("x1", ms), "lnbf", p=ms % 2)
            transpose8(s_lnbf, "lnbf")
            P.op("v", lambda e, ms=ms: e.tensor_copy(
                s_ln2t[:, :, 128 * ms:128 * (ms + 1)],
                pbf[:].rearrange("p (dc t) -> p dc t", dc=8)),
                reads=[("tr", 0), ("tr", 1), "w2bf"], writes=[("ln2t", ms)])

        ln2_reads = [("ln2t", ms) for ms in range(4)]
        for hc in range(24):
            wb = hc % 4
            P.op("g", lambda e, hc=hc, wb=wb: e.dma_start(
                out=s_wg[wb], in_=wgr[:, :, 128 * hc:128 * (hc + 1)]),
                writes=[("wg", wb)], dma=True)
            P.op("g", lambda e, hc=hc, wb=wb: e.dma_start(
                out=s_wu[wb], in_=wur[:, :, 128 * hc:128 * (hc + 1)]),
                writes=[("wu", wb)], dma=True)

            def gmm(e, w, dst):
                last = None
                for dc in range(8):
                    last = e.matmul(dst, w[:, dc, :], s_ln2t[:, dc, :],
                                    start=(dc == 0), stop=(dc == 7))
                return last
            P.op("t", lambda e, wb=wb: gmm(e, s_wg[wb], pg_ps),
                 reads=[("wg", wb), *ln2_reads, "w2bf"], writes=["pb2"])
            P.op("t", lambda e, wb=wb: gmm(e, s_wu[wb], pu_ps),
                 reads=[("wu", wb), *ln2_reads, "w2bf"], writes=["pb3"])
            P.op("s", lambda e: e.activation(s_silu, pg_ps, AF.Silu, scale=INV127),
                 reads=["pb2", "w2bf"], writes=["silu"])
            P.op("v", lambda e, hc=hc: e.tensor_tensor(
                s_hbuf[:, 512 * hc:512 * (hc + 1)], pu_ps, s_silu, ALU.mult),
                reads=["pb3", "silu", "w2bf"], writes=[("h", hc)])

        h_reads = [("h", hc) for hc in range(24)]
        for eh in range(2):
            for hg in range(4):
                wb = hg % 2
                P.op("g", lambda e, hg=hg, eh=eh, wb=wb: e.dma_start(
                    out=s_wd[wb], in_=wdr[:, 6 * hg:6 * (hg + 1), 512 * eh:512 * (eh + 1)]),
                    writes=[("wd", wb)], dma=True)

                def dmm(e, hg=hg, wb=wb):
                    last = None
                    for hi in range(6):
                        hc = 6 * hg + hi
                        for ms in range(4):
                            last = e.matmul(
                                pd_ps[ms],
                                s_hbuf[:, 512 * hc + 128 * ms:512 * hc + 128 * (ms + 1)],
                                s_wd[wb][:, hi, :],
                                start=(hc == 0), stop=(hc == 23))
                    return last
                P.op("t", dmm, reads=[("wd", wb), *h_reads, "w2bf"],
                     writes=["pb2", "pb3", "pb4", "pb5"])
            for ms in range(4):
                P.op("v", lambda e, ms=ms, eh=eh: e.scalar_tensor_tensor(
                    s_ysb[:, D * ms + 512 * eh: D * ms + 512 * (eh + 1)],
                    pd_ps[ms], DOWN_SCALE,
                    s_x1c[:, D * ms + 512 * eh: D * ms + 512 * (eh + 1)],
                    ALU.mult, ALU.add),
                    reads=["pb" + str(2 + ms), ("x1", ms), "w2bf"], writes=[("ysb", ms, eh)])
        for ms in range(4):
            P.op("y", lambda e, cg=cg, ms=ms: e.dma_start(
                out=y_out[512 * cg + 128 * ms: 512 * cg + 128 * (ms + 1), :],
                in_=s_ysb[:, D * ms:D * (ms + 1)]),
                reads=[("ysb", ms, 0), ("ysb", ms, 1), "w2bf"],
                writes=[("yout", cg, ms)], dma=True)

    from contextlib import ExitStack
    with ExitStack() as ctx:
        sems = {}
        for name in sorted(P.cnt):
            if not name.startswith("#"):
                sems[name] = ctx.enter_context(nc.semaphore(name))
        block = ctx.enter_context(nc.Block())

        @block.tensor
        def _(t):
            _emit_one(P, "t", t, sems)

        @block.vector
        def _(v):
            _emit_one(P, "v", v, sems)

        @block.scalar
        def _(s):
            _emit_one(P, "s", s, sems)

        @block.gpsimd
        def _(g):
            _emit_one(P, "g", g, sems)

        @block.sync
        def _(y):
            _emit_one(P, "y", y, sems)

    return nc




def _tables(core):
    chunks = own_chunk_positions(core)
    hs = HD
    inv_freq = (1.0 / (10000.0 ** (np.arange(0, hs, 2, dtype=np.float32) / hs))
                ).astype(np.float32)
    own_tok = np.concatenate([np.arange(256 * p, 256 * (p + 1)) for p in chunks])
    pos = np.concatenate([np.arange(T), own_tok]).astype(np.float32)
    ang = (pos[:, None] * inv_freq[None, :]).astype(np.float32)
    cos = np.cos(ang).astype(np.float32)
    sin = np.sin(ang).astype(np.float32)
    rope = np.zeros((128, T + 2048), dtype=np.float32)
    for p in range(64):
        rope[p] = cos[:, p % 32]
    for p in range(64, 128):
        d = p - 64
        s = sin[:, d % 32]
        rope[p] = s if d < 32 else -s

    q = core % 4
    off_e, off_o = q, 3 - q
    masks = np.zeros((16, 128, 1024), dtype=np.float32)
    k_idx = np.arange(128)[:, None]
    c_idx = np.arange(256)[None, :]
    for r in range(16):
        pat_e = (128 * r + k_idx <= 256 * off_e + c_idx).astype(np.float32)
        pat_o = (128 * r + k_idx <= 1024 + 256 * off_o + c_idx).astype(np.float32)
        half = np.concatenate([pat_e, pat_o], axis=1)
        masks[r, :, 0:512] = half
        masks[r, :, 512:1024] = half
    return rope.astype(ml_dtypes.bfloat16), masks.astype(ml_dtypes.bfloat16), own_tok


def _host(a, dt):
    try:
        import jax
        a = jax.device_get(a)
    except Exception:
        pass
    return np.asarray(a).astype(dt, copy=False)


def kernel(L0_attn_x, L0_attn_W1, L0_attn_W2, L0_ffn_w_gate, L0_ffn_w_up,
           L0_ffn_w_down):
    x = np.ascontiguousarray(_host(L0_attn_x, np.float32))
    w1t = np.ascontiguousarray(_host(L0_attn_W1, np.float32).T)
    w2t = np.ascontiguousarray(_host(L0_attn_W2, np.float32).T)
    wgt = np.ascontiguousarray(_host(L0_ffn_w_gate, np.int32).T)
    wut = np.ascontiguousarray(_host(L0_ffn_w_up, np.int32).T)
    wdt = np.ascontiguousarray(_host(L0_ffn_w_down, np.int32).T)
    ident = np.eye(128).astype(ml_dtypes.bfloat16)
    ones64 = np.ones((1, 64), dtype=ml_dtypes.bfloat16)

    nc = build_program()
    in_maps = []
    own_toks = []
    for c in range(8):
        b = c // 4
        rope, masks, own_tok = _tables(c)
        own_toks.append(own_tok)
        in_maps.append(dict(
            x=np.ascontiguousarray(x[b]),
            xown=np.ascontiguousarray(x[b][own_tok]),
            w1t=w1t, w2t=w2t, wgt=wgt, wut=wut, wdt=wdt,
            ropetab=rope, maskst=masks, ident=ident, ones64=ones64))
    res = run_bass_kernel_spmd(nc, in_maps, core_ids=list(range(8)))

    out = np.empty((B, T, D), dtype=np.float32)
    for c in range(8):
        b = c // 4
        out[b, own_toks[c]] = res.results[c]["y"]
    return out
